# version 05
# baseline (speedup 1.0000x reference)
import numpy as np
import ml_dtypes
from contextlib import ExitStack

import concourse.bass as bass
import concourse.mybir as mybir
from concourse.bass_utils import run_bass_kernel_spmd

F32 = mybir.dt.float32
BF16 = mybir.dt.bfloat16
AF = mybir.ActivationFunctionType
ALU = mybir.AluOpType
AX = mybir.AxisListType
BF = ml_dtypes.bfloat16

D = 2048
KT = 16
TOKA = 2048
TOK = 1024
E = 16
KI = 11
MT = 16
NPASS = 20
GU_RING = 4
DN_RING = 4
N_GU = NPASS * 22 + 2 * KT
N_DN = NPASS * MT

DEBUG = False


class Builder:
    def __init__(self, nc):
        self.nc = nc
        self.q = {"sync": [], "gpsimd": [], "tensor": [], "vector": [], "scalar": []}
        self.cnt = {}
        self.sem = {}
        self.wm = {}

    def mksem(self, es, *names):
        for n in names:
            self.sem[n] = es.enter_context(self.nc.semaphore(n))
            self.cnt[n] = 0

    def emit(self, eng, f):
        self.q[eng].append(f)

    def wait(self, eng, tok):
        if tok is None:
            return
        sem_name, val = tok
        if val <= 0:
            return
        key = (eng, sem_name)
        if self.wm.get(key, 0) >= val:
            return
        self.wm[key] = val
        s = self.sem[sem_name]
        self.emit(eng, lambda e, s=s, v=val: e.wait_ge(s, v))

    def ins(self, eng, builder, sem_name=None, n=1):
        if sem_name is None:
            self.emit(eng, lambda e, b=builder: b(e))
            return None
        self.cnt[sem_name] += n
        s = self.sem[sem_name]

        def f(e, b=builder, s=s, nn=n):
            b(e).then_inc(s, nn)
        self.emit(eng, f)
        return (sem_name, self.cnt[sem_name])


def build_nc(debug=False):
    nc = bass.Bass("TRN2", target_bir_lowering=False)

    xatt = nc.dram_tensor("xatt", [KT, 128, TOKA], F32, kind="ExternalInput").ap()
    gus = nc.dram_tensor("gus", [N_GU, 128, D], F32, kind="ExternalInput").ap()
    dns = nc.dram_tensor("dns", [N_DN, 128, KI * 128], F32, kind="ExternalInput").ap()
    gw = nc.dram_tensor("gw", [128, 256], F32, kind="ExternalInput").ap()
    gv = nc.dram_tensor("gv", [128, 16], F32, kind="ExternalInput").ap()
    apar = nc.dram_tensor("apar", [128, 1], F32, kind="ExternalInput").ap()
    seli = nc.dram_tensor("seli", [16, D], BF16, kind="ExternalInput").ap()
    id128 = nc.dram_tensor("id128", [128, 128], F32, kind="ExternalInput").ap()
    onesr = nc.dram_tensor("onesr", [1, 128], F32, kind="ExternalInput").ap()
    onesc = nc.dram_tensor("onesc", [128, 1], F32, kind="ExternalInput").ap()
    out = nc.dram_tensor("out", [KT, 128, TOK], F32, kind="ExternalOutput").ap()
    xnsp = nc.dram_tensor("xnspill", [KT, 128, TOK], BF16, kind="Internal").ap()
    dbg = {}
    if debug:
        dbg["xn"] = nc.dram_tensor("dbg_xn", [KT, 128, TOKA], BF16, kind="ExternalOutput").ap()
        dbg["comb"] = nc.dram_tensor("dbg_comb", [16, TOK], F32, kind="ExternalOutput").ap()
        dbg["acc"] = nc.dram_tensor("dbg_acc", [KT, 128, TOK], F32, kind="ExternalOutput").ap()
        dbg["h"] = nc.dram_tensor("dbg_h", [KT, 128, TOKA], F32, kind="ExternalOutput").ap()
        dbg["logit"] = nc.dram_tensor("dbg_logit", [16, TOK], F32, kind="ExternalOutput").ap()

    B = Builder(nc)
    with ExitStack() as es:
        B.mksem(es, "din", "dgu", "ddn", "dsp", "ts", "vs", "ss", "ost")

        rg = [es.enter_context(nc.sbuf_tensor(f"rg{i}", [128, D], BF16)) for i in range(GU_RING)]
        s_xnown = [es.enter_context(nc.sbuf_tensor(f"xno{t}", [128, TOK], BF16)) for t in range(KT)]
        s_id = es.enter_context(nc.sbuf_tensor("sid", [128, 128], F32))
        s_onesr = es.enter_context(nc.sbuf_tensor("sonesr", [1, 128], F32))
        s_onesc = es.enter_context(nc.sbuf_tensor("sonesc", [128, 1], F32))
        s_apar = es.enter_context(nc.sbuf_tensor("sapar", [128, 1], F32))
        s_spa = es.enter_context(nc.sbuf_tensor("sspa", [128, 1], F32))
        s_scol = es.enter_context(nc.sbuf_tensor("sscol", [128, 1], F32))
        s_kc = es.enter_context(nc.sbuf_tensor("skc", [1, 4], F32))
        s_half = es.enter_context(nc.sbuf_tensor("shalf", [128, 1], F32))
        psg = [es.enter_context(nc.psum_tensor(f"psg{i}", [128, 512], F32)) for i in range(4)]
        psd = [es.enter_context(nc.psum_tensor(f"psd{i}", [128, TOK], F32)) for i in range(2)]
        psg_free = [None] * 4
        psd_free = [None] * 2

        gu_consumed = [None] * N_GU
        dn_consumed = [None] * N_DN

        def gu_ready(j):
            return ("dgu", 16 * (j + 1) if j < 4 else 16 * (j + 3))

        gw_tok = ("dgu", 16 * 5)
        gv_tok = ("dgu", 16 * 6)

        def dn_ready(j):
            return ("ddn", 16 * (j + 1))

        def mm_job(slot_ap, w_slab, kparts, rhs_of_k, extra_deps=()):
            for d in extra_deps:
                B.wait("tensor", d)
            tok = None
            for k in range(kparts):
                last = (k == kparts - 1)
                tok = B.ins("tensor",
                            lambda e, s=slot_ap, w=w_slab, k=k, r=rhs_of_k(k), la=last:
                            e.matmul(s, w[:, k * 128:(k + 1) * 128], r,
                                     start=(k == 0), stop=la),
                            "ts" if last else None, 1)
            return tok

        def mm_job2(slotA, slotB, w_slab, kparts, rhsA_of_k, rhsB_of_k, extra_deps=()):
            for d in extra_deps:
                B.wait("tensor", d)
            tok = None
            for k in range(kparts):
                last = (k == kparts - 1)
                B.ins("tensor",
                      lambda e, s=slotA, w=w_slab, k=k, r=rhsA_of_k(k), la=last:
                      e.matmul(s, w[:, k * 128:(k + 1) * 128], r,
                               start=(k == 0), stop=la, skip_group_check=True))
                tok = B.ins("tensor",
                            lambda e, s=slotB, w=w_slab, k=k, r=rhsB_of_k(k), la=last:
                            e.matmul(s, w[:, k * 128:(k + 1) * 128], r,
                                     start=(k == 0), stop=la, skip_group_check=True),
                            "ts" if last else None, 1)
            return tok

        esA = es.enter_context(ExitStack())
        s_x = [esA.enter_context(nc.sbuf_tensor(f"x{t}", [128, TOK], F32)) for t in range(KT)]
        s_x2 = [esA.enter_context(nc.sbuf_tensor(f"x2_{i}", [128, TOK], F32)) for i in range(2)]
        s_mean = esA.enter_context(nc.sbuf_tensor("smean", [1, TOK], F32))
        s_msq = esA.enter_context(nc.sbuf_tensor("smsq", [1, TOK], F32))
        s_var = esA.enter_context(nc.sbuf_tensor("svar", [1, TOK], F32))
        s_rstd = esA.enter_context(nc.sbuf_tensor("srstd", [1, TOK], F32))
        s_nmr = esA.enter_context(nc.sbuf_tensor("snmr", [1, TOK], F32))
        s_xnpre = [esA.enter_context(nc.sbuf_tensor(f"xnp{t}", [128, TOK], BF16)) for t in range(KT)]

        B.ins("vector", lambda e: e.memset(s_kc[0:1, 0:1], 1e-5), "vs", 1)
        B.ins("vector", lambda e: e.memset(s_kc[0:1, 1:2], 1.0 / D), "vs", 1)
        B.ins("vector", lambda e: e.memset(s_kc[0:1, 2:3], -0.5), "vs", 1)
        kc_tok = B.ins("vector", lambda e: e.memset(s_half[:], 0.5), "vs", 1)
        for (dst, src) in [(s_id, id128), (s_onesr, onesr), (s_onesc, onesc),
                           (s_apar, apar)]:
            B.ins("sync", lambda e, d=dst, s=src: e.dma_start(out=d[:], in_=s), "din", 16)
        aux_tok = ("din", B.cnt["din"])

        xn_tok = {}
        spill_tok = None
        s_xb = [esA.enter_context(nc.sbuf_tensor(f"xb{i}", [128, TOK], F32)) for i in range(4)]
        s_x2b = [esA.enter_context(nc.sbuf_tensor(f"x2b{i}", [128, TOK], F32)) for i in range(2)]
        s_at = [esA.enter_context(nc.sbuf_tensor(f"at{i}", [128, TOK], F32)) for i in range(2)]

        def emit_sums(make_load, get_x, sqb):
            sq_tok = [None] * KT
            x2mm_tok = [None] * KT
            sumx_tok = [None] * KT
            xtok = [None] * KT
            B.wait("tensor", aux_tok)
            if psd_free[0] is not None:
                B.wait("tensor", psd_free[0])
            if psd_free[1] is not None:
                B.wait("tensor", psd_free[1])
            for t in range(KT):
                ld_deps = []
                if t >= 4:
                    ld_deps = [sumx_tok[t - 4], x2mm_tok[t - 4]]
                xtok[t] = make_load(t, ld_deps)
                B.wait("scalar", xtok[t])
                if t >= 2:
                    B.wait("scalar", x2mm_tok[t - 2])
                sq_tok[t] = B.ins("scalar",
                                  lambda e, t=t, d=sqb[t % 2], xx=get_x(t):
                                  e.activation(out=d[:], in_=xx[:], func=AF.Square),
                                  "ss", 1)
                B.wait("tensor", xtok[t])
                B.wait("tensor", sq_tok[t])
                for c in range(2):
                    sumx_tok[t] = B.ins("tensor",
                                        lambda e, t=t, c=c, xx=get_x(t):
                                        e.matmul(psd[0][0:1, c * 512:(c + 1) * 512], s_onesc[:],
                                                 xx[:, c * 512:(c + 1) * 512],
                                                 start=(t == 0), stop=(t == KT - 1)),
                                        "ts", 1)
                    x2mm_tok[t] = B.ins("tensor",
                                        lambda e, t=t, c=c, d=sqb[t % 2]:
                                        e.matmul(psd[1][0:1, c * 512:(c + 1) * 512], s_onesc[:],
                                                 d[:, c * 512:(c + 1) * 512],
                                                 start=(t == 0), stop=(t == KT - 1)),
                                        "ts", 1)
            return sumx_tok[KT - 1], x2mm_tok[KT - 1]

        def emit_rows_bcasts(sumx_last, x2mm_last, bc_prev, apply_prev):
            B.wait("scalar", x2mm_last)
            B.wait("scalar", sumx_last)
            B.wait("scalar", kc_tok)
            if bc_prev is not None:
                B.wait("scalar", bc_prev)
            m_tok = B.ins("scalar", lambda e: e.activation(out=s_mean[:], in_=psd[0][0:1, :],
                                                           func=AF.Copy, scale=s_kc[0:1, 1:2]), "ss", 1)
            q_tok = B.ins("scalar", lambda e: e.activation(out=s_msq[:], in_=psd[1][0:1, :],
                                                           func=AF.Copy, scale=s_kc[0:1, 1:2]), "ss", 1)
            psd_free[0] = m_tok
            psd_free[1] = q_tok
            B.wait("vector", m_tok)
            B.wait("vector", q_tok)
            if bc_prev is not None:
                B.wait("vector", bc_prev)
            t1 = B.ins("vector", lambda e: e.tensor_mul(out=s_var[:], in0=s_mean[:],
                                                        in1=s_mean[:]), "vs", 1)
            B.wait("vector", t1)
            t2 = B.ins("vector", lambda e: e.tensor_sub(out=s_var[:], in0=s_msq[:],
                                                        in1=s_var[:]), "vs", 1)
            B.wait("scalar", t2)
            t3 = B.ins("scalar", lambda e: e.activation(out=s_rstd[:], in_=s_var[:],
                                                        func=AF.Ln, bias=s_kc[0:1, 0:1]), "ss", 1)
            B.wait("scalar", t3)
            t4 = B.ins("scalar", lambda e: e.activation(out=s_rstd[:], in_=s_rstd[:],
                                                        func=AF.Exp, scale=s_kc[0:1, 2:3]), "ss", 1)
            B.wait("vector", t4)
            t5 = B.ins("vector", lambda e: e.scalar_tensor_tensor(
                out=s_nmr[:], in0=s_mean[:], scalar=-1.0, in1=s_rstd[:],
                op0=ALU.mult, op1=ALU.mult), "vs", 1)
            bc_tok = [None] * 4
            for c in range(2):
                B.wait("tensor", t4)
                if apply_prev is not None:
                    B.wait("tensor", apply_prev)
                bc_tok[c] = B.ins("tensor",
                                  lambda e, c=c: e.matmul(psg[c][:, 0:512], s_onesr[:],
                                                          s_rstd[0:1, c * 512:(c + 1) * 512],
                                                          start=True, stop=True), "ts", 1)
            for c in range(2):
                B.wait("tensor", t5)
                bc_tok[2 + c] = B.ins("tensor",
                                      lambda e, c=c: e.matmul(psg[2 + c][:, 0:512], s_onesr[:],
                                                              s_nmr[0:1, c * 512:(c + 1) * 512],
                                                              start=True, stop=True), "ts", 1)
            return bc_tok

        def emit_apply_tile(half, t, xx, xtok_t, bc_tok):
            dst = s_xnpre[t] if half == 0 else s_xnown[t]
            B.wait("vector", xtok_t)
            if (half, t - 2) in xn_tok:
                B.wait("vector", xn_tok[(half, t - 2)])
            elif (half - 1, KT - 2 + t) in xn_tok:
                B.wait("vector", xn_tok[(half - 1, KT - 2 + t)])
            for c in range(2):
                B.wait("vector", bc_tok[c])
                B.wait("vector", bc_tok[2 + c])
                u1 = B.ins("vector",
                           lambda e, t=t, c=c, tmp=s_at[t % 2], xx=xx:
                           e.tensor_mul(out=tmp[:, 0:512],
                                        in0=xx[:, c * 512:(c + 1) * 512],
                                        in1=psg[c][:, 0:512]), "vs", 1)
                B.wait("vector", u1)
                u2 = B.ins("vector",
                           lambda e, t=t, c=c, tmp=s_at[t % 2], d=dst:
                           e.tensor_add(out=d[:, c * 512:(c + 1) * 512], in0=tmp[:, 0:512],
                                        in1=psg[2 + c][:, 0:512]), "vs", 1)
            xn_tok[(half, t)] = u2
            if debug:
                B.wait("sync", u2)
                B.ins("sync", lambda e, t=t, h=half, d=dst:
                      e.dma_start(out=dbg["xn"][t, :, h * TOK:(h + 1) * TOK], in_=d[:]),
                      "ost", 16)

        xtok0 = [B.ins("sync",
                       lambda e, t=t, d=s_x[t]: e.dma_start(out=d[:], in_=xatt[t, :, 0:TOK]),
                       "din", 16) for t in range(KT)]
        sx0, x20 = emit_sums(lambda t, deps: xtok0[t], lambda t: s_x[t], s_x2)
        bc0 = emit_rows_bcasts(sx0, x20, None, None)
        xtok1 = [None] * KT

        def load1(t, deps):
            for d in deps:
                B.wait("sync", d)
            tok = B.ins("sync",
                        lambda e, t=t, d=s_xb[t % 4]:
                        e.dma_start(out=d[:], in_=xatt[t, :, TOK:TOKA]), "din", 16)
            xtok1[t] = tok
            return tok

        sx1, x21 = emit_sums(load1, lambda t: s_xb[t % 4], s_x2b)
        for t in range(KT):
            emit_apply_tile(0, t, s_x[t], xtok0[t], bc0)
        for t in range(KT):
            B.wait("sync", xn_tok[(0, t)])
            spill_tok = B.ins("sync",
                              lambda e, t=t: e.dma_start(out=xnsp[t], in_=s_xnpre[t][:]),
                              "dsp", 16)
        bc1 = emit_rows_bcasts(sx1, x21, bc0[3], xn_tok[(0, KT - 1)])
        for t in range(KT):
            B.wait("sync", x21)
            if t >= 4:
                B.wait("sync", xn_tok[(1, t - 4)])
            xt = B.ins("sync",
                       lambda e, t=t, d=s_xb[t % 4]:
                       e.dma_start(out=d[:], in_=xatt[t, :, TOK:TOKA]), "din", 16)
            emit_apply_tile(1, t, s_xb[t % 4], xt, bc1)
        for c in range(4):
            psg_free[c] = xn_tok[(1, KT - 1)]

        B.wait("scalar", aux_tok)
        p1 = B.ins("scalar", lambda e: e.activation(out=s_spa[:], in_=s_apar[:], func=AF.Abs), "ss", 1)
        B.wait("scalar", p1)
        p2 = B.ins("scalar", lambda e: e.activation(out=s_spa[:], in_=s_spa[:], func=AF.Exp,
                                                    scale=-1.0), "ss", 1)
        B.wait("scalar", p2)
        p3 = B.ins("scalar", lambda e: e.activation(out=s_spa[:], in_=s_spa[:], func=AF.Ln,
                                                    bias=1.0), "ss", 1)
        B.wait("scalar", p3)
        p4 = B.ins("scalar", lambda e: e.activation(out=s_scol[:], in_=s_apar[:], func=AF.Relu), "ss", 1)
        B.wait("scalar", p4)
        p5 = B.ins("scalar", lambda e: e.activation(out=s_scol[:], in_=s_scol[:], func=AF.Relu,
                                                    bias=s_spa[:, 0:1]), "ss", 1)
        B.wait("vector", p5)
        spa_tok = B.ins("vector", lambda e: e.tensor_scalar_mul(s_scol[:], s_scol[:], -8.0), "vs", 1)

        xn_all = xn_tok[(1, KT - 1)]
        spill_done = ("dsp", 16 * KT)
        esA.close()
        s_acc = [es.enter_context(nc.sbuf_tensor(f"acc{m}", [128, TOK], F32)) for m in range(MT)]
        esW = ExitStack()
        rd = [esW.enter_context(nc.sbuf_tensor(f"rd{i}", [128, KI * 128], BF16)) for i in range(DN_RING)]
        s_gw = esW.enter_context(nc.sbuf_tensor("sgw", [128, 256], BF16))
        s_gv = esW.enter_context(nc.sbuf_tensor("sgv", [128, 16], BF16))
        s_sel = esW.enter_context(nc.sbuf_tensor("ssel", [16, D], BF16))
        s_comb = esW.enter_context(nc.sbuf_tensor("scomb", [16, TOK], BF16))
        s_sgb = esW.enter_context(nc.sbuf_tensor("ssgb", [128, TOK], BF16))
        s_combb = esW.enter_context(nc.sbuf_tensor("scombb", [128, TOK], BF16))

        esB = es.enter_context(ExitStack())
        s_hp = [esB.enter_context(nc.sbuf_tensor(f"hp{i}", [128, TOK], BF16)) for i in range(KI)]
        s_silu = [esB.enter_context(nc.sbuf_tensor(f"sil{i}", [128, 512], F32)) for i in range(2)]
        s_ut = [esB.enter_context(nc.sbuf_tensor(f"ut{i}", [128, 512], F32)) for i in range(2)]
        s_lg = esB.enter_context(nc.sbuf_tensor("slg", [16, TOK], F32))
        s_sg = esB.enter_context(nc.sbuf_tensor("ssg", [1, TOK], F32))
        s_ltc = esB.enter_context(nc.sbuf_tensor("sltc", [128, 128], F32))
        s_q = esB.enter_context(nc.sbuf_tensor("sq", [128, 128], F32))
        s_w = esB.enter_context(nc.sbuf_tensor("sw", [128, 128], F32))
        s_msk = esB.enter_context(nc.sbuf_tensor("smsk", [128, 128], F32))
        s_ctc = esB.enter_context(nc.sbuf_tensor("sctc", [128, 128], F32))
        s_nmax = esB.enter_context(nc.sbuf_tensor("snmax", [128, 8], F32))
        s_sum8 = esB.enter_context(nc.sbuf_tensor("ssum8", [128, 8], F32))
        s_rs8 = esB.enter_context(nc.sbuf_tensor("srs8", [128, 8], F32))
        s_m8 = esB.enter_context(nc.sbuf_tensor("sm8", [128, 8], F32))
        s_tr8 = esB.enter_context(nc.sbuf_tensor("str8", [128, 8], F32))
        if debug:
            s_combf = esB.enter_context(nc.sbuf_tensor("scombf", [16, TOK], F32))

        for eng in ("vector", "scalar", "tensor", "sync"):
            B.wait(eng, spill_done)
        sel_tok = B.ins("sync", lambda e: e.dma_start(out=s_sel[:], in_=seli), "din", 16)

        def xn_rhs(k, c):
            return s_xnown[k][:, c * 512:(c + 1) * 512]

        B.wait("tensor", gw_tok)
        if psd_free[0] is not None:
            B.wait("tensor", psd_free[0])
        lg_tok = None
        for c in range(2):
            for k in range(KT):
                B.wait("tensor", xn_tok[(1, k)])
                lg_tok = B.ins("tensor",
                               lambda e, c=c, k=k:
                               e.matmul(psd[0][0:16, c * 512:(c + 1) * 512],
                                        s_gw[:, k * 16:(k + 1) * 16], xn_rhs(k, c),
                                        start=(k == 0), stop=(k == KT - 1)),
                               "ts" if k == KT - 1 else None, 1)
        B.wait("scalar", lg_tok)
        lg_sb = B.ins("scalar", lambda e: e.copy(out=s_lg[:], in_=psd[0][0:16, :]), "ss", 1)
        psd_free[0] = lg_sb
        if debug:
            B.wait("sync", lg_sb)
            B.ins("sync", lambda e: e.dma_start(out=dbg["logit"], in_=s_lg[:]), "ost", 16)
        B.wait("tensor", gv_tok)
        if psd_free[1] is not None:
            B.wait("tensor", psd_free[1])
        sg_tok = None
        for c in range(2):
            for k in range(KT):
                B.wait("tensor", xn_tok[(1, k)])
                sg_tok = B.ins("tensor",
                               lambda e, c=c, k=k:
                               e.matmul(psd[1][0:1, c * 512:(c + 1) * 512],
                                        s_gv[:, k:k + 1], xn_rhs(k, c),
                                        start=(k == 0), stop=(k == KT - 1)),
                               "ts" if k == KT - 1 else None, 1)
        B.wait("scalar", sg_tok)
        sg_row = B.ins("scalar", lambda e: e.activation(out=s_sg[:], in_=psd[1][0:1, :],
                                                        func=AF.Sigmoid), "ss", 1)
        psd_free[1] = sg_row
        sgb_tok = None
        for c in range(2):
            B.wait("tensor", sg_row)
            if psg_free[c] is not None:
                B.wait("tensor", psg_free[c])
            bt = B.ins("tensor", lambda e, c=c:
                       e.matmul(psg[c][:, 0:512], s_onesr[:], s_sg[0:1, c * 512:(c + 1) * 512],
                                start=True, stop=True), "ts", 1)
            B.wait("scalar", bt)
            sgb_tok = B.ins("scalar", lambda e, c=c:
                            e.copy(out=s_sgb[:, c * 512:(c + 1) * 512], in_=psg[c][:, 0:512]),
                            "ss", 1)
            psg_free[c] = sgb_tok

        ltc_tok = []
        for i in range(8):
            sl = i % 4
            B.wait("tensor", lg_sb)
            if psg_free[sl] is not None:
                B.wait("tensor", psg_free[sl])
            tt = B.ins("tensor",
                       lambda e, i=i, sl=sl:
                       e.transpose(psg[sl][:, 0:16], s_lg[0:16, i * 128:(i + 1) * 128],
                                   s_id[0:16, 0:16]), "ts", 1)
            B.wait("vector", tt)
            ct = B.ins("vector",
                       lambda e, i=i, sl=sl:
                       e.tensor_copy(out=s_ltc[:, i * 16:(i + 1) * 16], in_=psg[sl][:, 0:16]),
                       "vs", 1)
            psg_free[sl] = ct
            ltc_tok.append(ct)
        mtok = B.ins("vector", lambda e: e.memset(s_tr8[:], -2.0), "vs", 1)
        comb_tc_tok = []
        for i in range(8):
            isl = slice(i * 16, (i + 1) * 16)
            B.wait("vector", ltc_tok[i])
            r1 = B.ins("vector", lambda e, isl=isl, i=i:
                       e.reduce_max(out=s_nmax[:, i:i + 1], in_=s_ltc[:, isl], axis=AX.X,
                                    negate=True), "vs", 1)
            B.wait("scalar", r1)
            r2 = B.ins("scalar", lambda e, isl=isl, i=i:
                       e.activation(out=s_q[:, isl], in_=s_ltc[:, isl], func=AF.Exp,
                                    bias=s_nmax[:, i:i + 1]), "ss", 1)
            B.wait("vector", r2)
            r3 = B.ins("vector", lambda e, isl=isl:
                       e.max(out=s_m8[:], in_=s_q[:, isl]), "vs", 1)
            B.wait("vector", r3)
            B.wait("vector", mtok)
            r4 = B.ins("vector", lambda e: e.tensor_copy(out=s_tr8[:, 0:4], in_=s_m8[:, 0:4]),
                       "vs", 1)
            B.wait("vector", r4)
            r5 = B.ins("vector", lambda e, isl=isl:
                       e.match_replace(out=s_w[:, isl], in_to_replace=s_tr8[:],
                                       in_values=s_q[:, isl], imm_value=-1.0), "vs", 1)
            B.wait("vector", r5)
            r6 = B.ins("vector", lambda e, isl=isl:
                       e.tensor_tensor(out=s_msk[:, isl], in0=s_q[:, isl], in1=s_w[:, isl],
                                       op=ALU.not_equal), "vs", 1)
            B.wait("vector", r6)
            r7 = B.ins("vector", lambda e, isl=isl:
                       e.tensor_mul(out=s_ctc[:, isl], in0=s_q[:, isl], in1=s_msk[:, isl]),
                       "vs", 1)
            B.wait("vector", r7)
            r8 = B.ins("vector", lambda e, isl=isl, i=i:
                       e.reduce_sum(out=s_sum8[:, i:i + 1], in_=s_ctc[:, isl], axis=AX.X),
                       "vs", 1)
            B.wait("vector", r8)
            r9 = B.ins("vector", lambda e, i=i:
                       e.reciprocal(out=s_rs8[:, i:i + 1], in_=s_sum8[:, i:i + 1]), "vs", 1)
            B.wait("vector", r9)
            r10 = B.ins("vector", lambda e, isl=isl, i=i:
                        e.tensor_scalar_mul(s_ctc[:, isl], s_ctc[:, isl], s_rs8[:, i:i + 1]),
                        "vs", 1)
            comb_tc_tok.append(r10)
        if psd_free[0] is not None:
            B.wait("tensor", psd_free[0])
        tb_last = None
        for i in range(8):
            B.wait("tensor", comb_tc_tok[i])
            tb_last = B.ins("tensor",
                            lambda e, i=i:
                            e.transpose(psd[0][0:16, i * 128:(i + 1) * 128],
                                        s_ctc[:, i * 16:(i + 1) * 16], s_id[:]), "ts", 1)
        B.wait("scalar", tb_last)
        comb_tok = B.ins("scalar", lambda e: e.copy(out=s_comb[:], in_=psd[0][0:16, :]), "ss", 1)
        psd_free[0] = comb_tok
        if debug:
            dcf = B.ins("scalar", lambda e: e.copy(out=s_combf[:], in_=psd[0][0:16, :]), "ss", 1)
            psd_free[0] = dcf
            B.wait("sync", dcf)
            B.ins("sync", lambda e: e.dma_start(out=dbg["comb"], in_=s_combf[:]), "ost", 16)

        acc_tok = [None] * MT
        silu_cons = [None, None]
        ut_cons = [None, None]
        gslot_i = 0
        dslot_i = 0
        gu_j = 0
        dn_j = 0
        combb_tok = None

        for p in range(NPASS):
            is_exp = p >= 4
            eidx = p - 4
            if is_exp:
                for c in range(2):
                    sl = dslot_i % 2
                    dslot_i += 1
                    B.wait("tensor", comb_tok)
                    B.wait("tensor", sel_tok)
                    if psd_free[sl] is not None:
                        B.wait("tensor", psd_free[sl])
                    bt = B.ins("tensor",
                               lambda e, c=c, sl=sl, eidx=eidx:
                               e.matmul(psd[sl][:, 0:512],
                                        s_sel[:, eidx * 128:(eidx + 1) * 128],
                                        s_comb[0:16, c * 512:(c + 1) * 512],
                                        start=True, stop=True), "ts", 1)
                    B.wait("scalar", bt)
                    combb_tok = B.ins("scalar",
                                      lambda e, c=c, sl=sl:
                                      e.copy(out=s_combb[:, c * 512:(c + 1) * 512],
                                             in_=psd[sl][:, 0:512]), "ss", 1)
                    psd_free[sl] = combb_tok
            h_tok = [None] * KI
            for i in range(KI):
                g_slab_j = gu_j
                u_slab_j = gu_j + 1
                gu_j += 2
                slg0, slg1 = gslot_i % 4, (gslot_i + 1) % 4
                slu0, slu1 = (gslot_i + 2) % 4, (gslot_i + 3) % 4
                gslot_i += 4
                deps = [xn_all, gu_ready(g_slab_j)]
                for s_ in (slg0, slg1):
                    if psg_free[s_] is not None:
                        deps.append(psg_free[s_])
                gt = mm_job2(psg[slg0][:, 0:512], psg[slg1][:, 0:512],
                             rg[g_slab_j % GU_RING], KT,
                             lambda k: xn_rhs(k, 0), lambda k: xn_rhs(k, 1),
                             extra_deps=deps)
                gu_consumed[g_slab_j] = gt
                deps = [gu_ready(u_slab_j)]
                for s_ in (slu0, slu1):
                    if psg_free[s_] is not None:
                        deps.append(psg_free[s_])
                ut = mm_job2(psg[slu0][:, 0:512], psg[slu1][:, 0:512],
                             rg[u_slab_j % GU_RING], KT,
                             lambda k: xn_rhs(k, 0), lambda k: xn_rhs(k, 1),
                             extra_deps=deps)
                gu_consumed[u_slab_j] = ut
                for c in range(2):
                    slg_ = slg0 if c == 0 else slg1
                    slu_ = slu0 if c == 0 else slu1
                    B.wait("scalar", gt)
                    B.wait("scalar", silu_cons[c])
                    sv = B.ins("scalar",
                               lambda e, slg_=slg_, b=s_silu[c]:
                               e.activation(out=b[:], in_=psg[slg_][:, 0:512], func=AF.Silu),
                               "ss", 1)
                    psg_free[slg_] = sv
                    B.wait("vector", ut)
                    B.wait("vector", sv)
                    if is_exp:
                        B.wait("vector", ut_cons[c])
                        uv = B.ins("vector",
                                   lambda e, slu_=slu_, b=s_silu[c], o=s_ut[c]:
                                   e.scalar_tensor_tensor(out=o[:], in0=psg[slu_][:, 0:512],
                                                          scalar=0.0, in1=b[:],
                                                          op0=ALU.bypass, op1=ALU.mult),
                                   "vs", 1)
                        psg_free[slu_] = uv
                        silu_cons[c] = uv
                        B.wait("vector", uv)
                        B.wait("vector", combb_tok)
                        hv = B.ins("vector",
                                   lambda e, i=i, c=c, o=s_ut[c]:
                                   e.tensor_mul(out=s_hp[i][:, c * 512:(c + 1) * 512],
                                                in0=o[:], in1=s_combb[:, c * 512:(c + 1) * 512]),
                                   "vs", 1)
                        ut_cons[c] = hv
                    else:
                        hv = B.ins("vector",
                                   lambda e, i=i, c=c, slu_=slu_, b=s_silu[c]:
                                   e.scalar_tensor_tensor(
                                       out=s_hp[i][:, c * 512:(c + 1) * 512],
                                       in0=psg[slu_][:, 0:512], scalar=0.0, in1=b[:],
                                       op0=ALU.bypass, op1=ALU.mult), "vs", 1)
                        psg_free[slu_] = hv
                        silu_cons[c] = hv
                    h_tok[i] = hv
            for m in range(MT):
                d_slab_j = dn_j
                dn_j += 1
                sl = dslot_i % 2
                dslot_i += 1
                deps = [dn_ready(d_slab_j)] + h_tok
                if psd_free[sl] is not None:
                    deps.append(psd_free[sl])
                for d in deps:
                    B.wait("tensor", d)
                dtok = None
                for k in range(KI):
                    for c in range(2):
                        last = (k == KI - 1)
                        dtok = B.ins("tensor",
                                     lambda e, sl=sl, k=k, c=c, w=rd[d_slab_j % DN_RING], la=last:
                                     e.matmul(psd[sl][:, c * 512:(c + 1) * 512],
                                              w[:, k * 128:(k + 1) * 128],
                                              s_hp[k][:, c * 512:(c + 1) * 512],
                                              start=(k == 0), stop=la,
                                              skip_group_check=True),
                                     "ts" if (c == 1 and last) else None, 1)
                dn_consumed[d_slab_j] = dtok
                B.wait("vector", dtok)
                if acc_tok[m] is not None:
                    B.wait("vector", acc_tok[m])
                if p == 0:
                    ev = B.ins("vector",
                               lambda e, m=m, sl=sl: e.tensor_copy(out=s_acc[m][:],
                                                                   in_=psd[sl][:]), "vs", 1)
                else:
                    ev = B.ins("vector",
                               lambda e, m=m, sl=sl: e.tensor_add(out=s_acc[m][:],
                                                                  in0=psd[sl][:],
                                                                  in1=s_acc[m][:]), "vs", 1)
                acc_tok[m] = ev
                psd_free[sl] = ev
            if p == 3:
                for m in range(MT):
                    B.wait("vector", acc_tok[m])
                    B.wait("vector", sgb_tok)
                    acc_tok[m] = B.ins("vector",
                                       lambda e, m=m: e.tensor_mul(out=s_acc[m][:],
                                                                   in0=s_acc[m][:],
                                                                   in1=s_sgb[:]), "vs", 1)
        assert gu_j == NPASS * 22 and dn_j == N_DN
        ffn_ts_done = ("ts", B.cnt["ts"])

        dbg_acc_tok = None
        if debug:
            for m in range(MT):
                B.wait("sync", acc_tok[m])
                dbg_acc_tok = B.ins("sync",
                                    lambda e, m=m: e.dma_start(out=dbg["acc"][m], in_=s_acc[m][:]),
                                    "ost", 16)

        for m in range(MT):
            B.wait("scalar", acc_tok[m])
            B.wait("scalar", dbg_acc_tok)
            acc_tok[m] = B.ins("scalar",
                               lambda e, m=m: e.activation(out=s_acc[m][:], in_=s_acc[m][:],
                                                           func=AF.Silu), "ss", 1)

        for j in range(4):
            B.ins("gpsimd",
                  lambda e, j=j, sl=rg[j % GU_RING]: e.dma_start(out=sl[:], in_=gus[j]),
                  "dgu", 16)
        B.wait("gpsimd", spill_done)
        B.ins("gpsimd", lambda e: e.dma_start(out=s_gw[:], in_=gw), "dgu", 16)
        B.ins("gpsimd", lambda e: e.dma_start(out=s_gv[:], in_=gv), "dgu", 16)
        order = []
        for p in range(NPASS):
            for j in range(p * 22, (p + 1) * 22):
                if j >= 4:
                    order.append(("gu", j))
            for j in range(p * MT, (p + 1) * MT):
                order.append(("dn", j))
        for kind, j in order:
            if kind == "gu":
                B.wait("gpsimd", gu_consumed[j - GU_RING] if j >= GU_RING else None)
                B.ins("gpsimd",
                      lambda e, j=j, sl=rg[j % GU_RING]: e.dma_start(out=sl[:], in_=gus[j]),
                      "dgu", 16)
            else:
                B.wait("gpsimd", dn_consumed[j - DN_RING] if j >= DN_RING else None)
                B.ins("gpsimd",
                      lambda e, j=j, sl=rd[j % DN_RING]: e.dma_start(out=sl[:], in_=dns[j]),
                      "ddn", 16)
        esB.close()
        esW.close()

        esC = es.enter_context(ExitStack())
        s_xnpre2 = [esC.enter_context(nc.sbuf_tensor(f"xnq{t}", [128, TOK], BF16)) for t in range(KT)]
        s_gx = [esC.enter_context(nc.sbuf_tensor(f"sgx{i}", [128, TOKA], F32)) for i in range(2)]
        s_ga = [esC.enter_context(nc.sbuf_tensor(f"sga{i}", [128, TOKA], F32)) for i in range(2)]
        s_t1 = esC.enter_context(nc.sbuf_tensor("st1", [128, TOKA], F32))
        s_xo = [esC.enter_context(nc.sbuf_tensor("sxo0", [128, TOK], F32))] * 2
        s_ob = [esC.enter_context(nc.sbuf_tensor(f"sob{i}", [128, TOK], F32)) for i in range(2)]

        B.wait("sync", ffn_ts_done)
        B.wait("sync", acc_tok[MT - 1])

        pre_tok = []
        for t in range(KT):
            B.wait("sync", spill_tok)
            pre_tok.append(B.ins("sync",
                                 lambda e, t=t: e.dma_start(out=s_xnpre2[t][:], in_=xnsp[t]),
                                 "din", 16))

        def xn_rhs_c(k, c):
            if c < 2:
                return s_xnpre2[k][:, (c % 2) * 512:((c % 2) + 1) * 512]
            return s_xnown[k][:, (c % 2) * 512:((c % 2) + 1) * 512]

        scan_tok = [None] * KT
        prev_o1_tok = None
        xo_cons = [None, None]
        ob_store = [None, None]
        for t in range(KT):
            gx_slab = NPASS * 22 + 2 * t
            ga_slab = NPASS * 22 + 2 * t + 1
            rb = t % 2
            gxb, gab = s_gx[rb], s_ga[rb]
            buf_free = scan_tok[t - 2] if t >= 2 else None
            B.wait("sync", xo_cons[0])
            xo_tok = B.ins("sync",
                           lambda e, t=t, rb=rb:
                           e.dma_start(out=s_xo[rb][:], in_=xatt[t, :, TOK:TOKA]),
                           "din", 16)
            gates_tok = None
            for part in range(2):
                slab_j = gx_slab if part == 0 else ga_slab
                dstt = gxb if part == 0 else gab
                for cc in range(2):
                    sl0 = gslot_i % 4
                    sl1 = (gslot_i + 1) % 4
                    gslot_i += 2
                    deps = [gu_ready(slab_j), pre_tok[KT - 1], xn_all]
                    for s_ in (sl0, sl1):
                        if psg_free[s_] is not None:
                            deps.append(psg_free[s_])
                    gt = mm_job2(psg[sl0][:, 0:512], psg[sl1][:, 0:512],
                                 rg[slab_j % GU_RING], KT,
                                 lambda k, cc=cc: xn_rhs_c(k, 2 * cc),
                                 lambda k, cc=cc: xn_rhs_c(k, 2 * cc + 1),
                                 extra_deps=deps)
                    if cc == 1:
                        gu_consumed[slab_j] = gt
                    for c in (2 * cc, 2 * cc + 1):
                        sl = sl0 if c % 2 == 0 else sl1
                        B.wait("scalar", gt)
                        B.wait("scalar", buf_free)
                        gates_tok = B.ins("scalar",
                                          lambda e, sl=sl, c=c, d=dstt:
                                          e.activation(out=d[:, c * 512:(c + 1) * 512],
                                                       in_=psg[sl][:, 0:512], func=AF.Sigmoid),
                                          "ss", 1)
                        psg_free[sl] = gates_tok
            B.wait("scalar", gates_tok)
            B.wait("scalar", spa_tok)
            a_tok = B.ins("scalar",
                          lambda e, gab=gab: e.activation(out=gab[:], in_=gab[:], func=AF.Exp,
                                                          scale=s_scol[:, 0:1]), "ss", 1)
            B.wait("vector", a_tok)
            if prev_o1_tok is not None:
                B.wait("vector", prev_o1_tok)
            c1 = B.ins("vector", lambda e, gab=gab: e.tensor_mul(out=s_t1[:], in0=gab[:],
                                                                 in1=gab[:]), "vs", 1)
            B.wait("vector", c1)
            c2 = B.ins("vector", lambda e: e.tensor_scalar(out=s_t1[:], in0=s_t1[:],
                                                           scalar1=-1.0, scalar2=1.0,
                                                           op0=ALU.mult, op1=ALU.add), "vs", 1)
            B.wait("vector", c2)
            c3 = B.ins("vector", lambda e: e.tensor_scalar_max(s_t1[:], s_t1[:], 1e-38), "vs", 1)
            B.wait("scalar", c3)
            c4 = B.ins("scalar", lambda e: e.activation(out=s_t1[:], in_=s_t1[:], func=AF.Ln),
                       "ss", 1)
            B.wait("scalar", c4)
            B.wait("scalar", kc_tok)
            c5 = B.ins("scalar", lambda e: e.activation(out=s_t1[:], in_=s_t1[:], func=AF.Exp,
                                                        scale=s_half[:, 0:1]), "ss", 1)
            B.wait("vector", c5)
            b1 = B.ins("vector", lambda e, gxb=gxb: e.tensor_mul(out=gxb[:], in0=s_t1[:],
                                                                 in1=gxb[:]), "vs", 1)
            B.wait("vector", b1)
            B.wait("vector", pre_tok[t])
            b2 = B.ins("vector", lambda e, t=t, gxb=gxb:
                       e.tensor_mul(out=gxb[:, 0:TOK], in0=gxb[:, 0:TOK],
                                    in1=s_xnpre2[t][:]), "vs", 1)
            B.wait("vector", b2)
            b3 = B.ins("vector", lambda e, t=t, gxb=gxb:
                       e.tensor_mul(out=gxb[:, TOK:TOKA], in0=gxb[:, TOK:TOKA],
                                    in1=s_xnown[t][:]), "vs", 1)
            B.wait("vector", b3)
            sc = B.ins("vector", lambda e, gab=gab, gxb=gxb:
                       e.tensor_tensor_scan(out=s_t1[:], data0=gab[:], data1=gxb[:],
                                            initial=0.0, op0=ALU.mult, op1=ALU.add), "vs", 1)
            scan_tok[t] = sc
            if debug:
                B.wait("sync", sc)
                B.ins("sync", lambda e, t=t: e.dma_start(out=dbg["h"][t], in_=s_t1[:]), "ost", 16)
            B.wait("vector", sc)
            B.wait("vector", acc_tok[t])
            if ob_store[rb] is not None:
                B.wait("vector", ob_store[rb])
            o1 = B.ins("vector", lambda e, t=t, rb=rb:
                       e.tensor_mul(out=s_ob[rb][:], in0=s_t1[:, TOK:TOKA], in1=s_acc[t][:]),
                       "vs", 1)
            prev_o1_tok = o1
            B.wait("vector", o1)
            B.wait("vector", xo_tok)
            o2 = B.ins("vector", lambda e, rb=rb:
                       e.tensor_add(out=s_ob[rb][:], in0=s_ob[rb][:], in1=s_xo[rb][:]), "vs", 1)
            xo_cons[0] = o2
            B.wait("sync", o2)
            ob_store[rb] = B.ins("sync",
                                 lambda e, t=t, rb=rb: e.dma_start(out=out[t], in_=s_ob[rb][:]),
                                 "ost", 16)
        assert gu_j + 2 * KT == N_GU

        B.emit("sync", lambda e, v=B.cnt["ost"], s=B.sem["ost"]: e.wait_ge(s, v))

        for j in range(NPASS * 22, N_GU):
            B.wait("gpsimd", gu_consumed[j - GU_RING])
            B.ins("gpsimd",
                  lambda e, j=j, sl=rg[j % GU_RING]: e.dma_start(out=sl[:], in_=gus[j]),
                  "dgu", 16)

        block = es.enter_context(nc.Block())

        @block.sync
        def _(sync):
            for f in B.q["sync"]:
                f(sync)

        @block.gpsimd
        def _(g):
            for f in B.q["gpsimd"]:
                f(g)

        @block.tensor
        def _(te):
            for f in B.q["tensor"]:
                f(te)

        @block.vector
        def _(v):
            for f in B.q["vector"]:
                f(v)

        @block.scalar
        def _(s):
            for f in B.q["scalar"]:
                f(s)

    return nc


def _slab(W):
    Mo, Ki = W.shape
    A = W.reshape(Mo // 128, 128, Ki // 128, 128)
    return np.ascontiguousarray(A.transpose(0, 3, 2, 1)).reshape(Mo // 128, 128, Ki)


_BUILD_CACHE = {}


def _get_nc(debug=False):
    key = bool(debug)
    if key not in _BUILD_CACHE:
        _BUILD_CACHE[key] = build_nc(debug=debug)
    return _BUILD_CACHE[key]


def prep_in_maps(L0_attn_x, L0_attn_W_in, L0_attn_a_param, L0_ffn_gate_w,
                 L0_ffn_expert_gate_up, L0_ffn_expert_down, L0_ffn_shared_gate_up,
                 L0_ffn_shared_down, L0_ffn_shared_gate_vec):
    f32 = np.float32
    x = np.asarray(L0_attn_x, f32)
    W_in = np.asarray(L0_attn_W_in, f32)
    a_param = np.asarray(L0_attn_a_param, f32).reshape(128, 1)
    gate_w = np.asarray(L0_ffn_gate_w, f32)
    egu = np.asarray(L0_ffn_expert_gate_up, f32)
    edn = np.asarray(L0_ffn_expert_down, f32)
    sgu = np.asarray(L0_ffn_shared_gate_up, f32)
    sdn = np.asarray(L0_ffn_shared_down, f32)
    gvec = np.asarray(L0_ffn_shared_gate_vec, f32)

    sgu_slabs = _slab(sgu)
    gu_list = []
    for g in range(4):
        for i in range(KI):
            gu_list.append(sgu_slabs[11 * g + i])
            gu_list.append(sgu_slabs[44 + 11 * g + i])
    for e in range(E):
        es_ = _slab(egu[e])
        for i in range(KI):
            gu_list.append(es_[i])
            gu_list.append(es_[11 + i])
    win_slabs = _slab(W_in)
    for t in range(KT):
        gu_list.append(win_slabs[t])
        gu_list.append(win_slabs[KT + t])
    gus = np.ascontiguousarray(np.stack(gu_list))
    assert gus.shape == (N_GU, 128, D)

    sdn_slabs = _slab(sdn)
    dn_list = []
    for g in range(4):
        for m in range(MT):
            dn_list.append(sdn_slabs[m][:, 1408 * g:1408 * (g + 1)])
    for e in range(E):
        ed = _slab(edn[e])
        for m in range(MT):
            dn_list.append(ed[m])
    dns = np.ascontiguousarray(np.stack(dn_list))
    assert dns.shape == (N_DN, 128, KI * 128)

    gw_slab = np.ascontiguousarray(
        gate_w.reshape(16, 16, 128).transpose(2, 1, 0)).reshape(128, 256)
    gv_slab = np.ascontiguousarray(gvec.reshape(16, 128).T)
    sel = np.zeros((16, D), dtype=BF)
    for e in range(E):
        sel[e, e * 128:(e + 1) * 128] = 1.0
    id128 = np.eye(128, dtype=f32)
    onesr = np.ones((1, 128), f32)
    onesc = np.ones((128, 1), f32)

    in_maps = []
    for core in range(8):
        c, h = core >> 1, core & 1
        own = x[c, TOK * h:TOK * (h + 1), :]
        pre = x[c, 0:TOK, :] if h == 1 else np.zeros((TOK, D), f32)
        xa = np.concatenate([pre, own], axis=0)
        xatt = np.ascontiguousarray(xa.T).reshape(KT, 128, TOKA)
        in_maps.append(dict(xatt=xatt, gus=gus, dns=dns, gw=gw_slab, gv=gv_slab,
                            apar=a_param, seli=sel, id128=id128, onesr=onesr,
                            onesc=onesc))
    return in_maps


def kernel(**inputs):
    in_maps = prep_in_maps(**inputs)
    nc = _get_nc(debug=DEBUG)
    res = run_bass_kernel_spmd(nc, in_maps, core_ids=list(range(8)))
    x = np.asarray(inputs["L0_attn_x"], np.float32)
    outp = np.empty_like(x)
    for core in range(8):
        c, h = core >> 1, core & 1
        o = np.asarray(res.results[core]["out"])
        outp[c, TOK * h:TOK * (h + 1), :] = o.reshape(D, TOK).T
    if DEBUG:
        kernel._last = res
    return outp


# version 06
# speedup vs baseline: 1.0683x; 1.0683x over previous
import numpy as np
import ml_dtypes
from contextlib import ExitStack

import concourse.bass as bass
import concourse.mybir as mybir
from concourse.bass_utils import run_bass_kernel_spmd

F32 = mybir.dt.float32
BF16 = mybir.dt.bfloat16
AF = mybir.ActivationFunctionType
ALU = mybir.AluOpType
AX = mybir.AxisListType
BF = ml_dtypes.bfloat16

D = 2048
KT = 16
TOK = 1024
E = 16
KI = 11
MT = 16
NPASS = 20
GU_RING = 4
DN_RING = 4
N_GU = NPASS * 22 + 2 * KT
N_DN = NPASS * MT
RGROUPS = [[0, 1], [2, 3], [4, 5], [6, 7]]


class Builder:
    def __init__(self, nc):
        self.nc = nc
        self.q = {"sync": [], "gpsimd": [], "tensor": [], "vector": [], "scalar": []}
        self.cnt = {}
        self.sem = {}
        self.wm = {}

    def mksem(self, es, *names):
        for n in names:
            self.sem[n] = es.enter_context(self.nc.semaphore(n))
            self.cnt[n] = 0

    def emit(self, eng, f):
        self.q[eng].append(f)

    def wait(self, eng, tok):
        if tok is None:
            return
        sem_name, val = tok
        if val <= 0:
            return
        key = (eng, sem_name)
        if self.wm.get(key, 0) >= val:
            return
        self.wm[key] = val
        s = self.sem[sem_name]
        self.emit(eng, lambda e, s=s, v=val: e.wait_ge(s, v))

    def ins(self, eng, builder, sem_name=None, n=1):
        if sem_name is None:
            self.emit(eng, lambda e, b=builder: b(e))
            return None
        self.cnt[sem_name] += n
        s = self.sem[sem_name]

        def f(e, b=builder, s=s, nn=n):
            b(e).then_inc(s, nn)
        self.emit(eng, f)
        return (sem_name, self.cnt[sem_name])


def build_nc():
    nc = bass.Bass("TRN2", target_bir_lowering=False)

    xin = nc.dram_tensor("xin", [KT, 128, TOK], F32, kind="ExternalInput").ap()
    gus = nc.dram_tensor("gus", [N_GU, 128, D], F32, kind="ExternalInput").ap()
    dns = nc.dram_tensor("dns", [N_DN, 128, KI * 128], F32, kind="ExternalInput").ap()
    gw = nc.dram_tensor("gw", [128, 256], F32, kind="ExternalInput").ap()
    gv = nc.dram_tensor("gv", [128, 16], F32, kind="ExternalInput").ap()
    apar = nc.dram_tensor("apar", [128, 1], F32, kind="ExternalInput").ap()
    cmask = nc.dram_tensor("cmask", [128, 1], F32, kind="ExternalInput").ap()
    seli = nc.dram_tensor("seli", [16, D], BF16, kind="ExternalInput").ap()
    id128 = nc.dram_tensor("id128", [128, 128], F32, kind="ExternalInput").ap()
    onesr = nc.dram_tensor("onesr", [1, 128], F32, kind="ExternalInput").ap()
    onesc = nc.dram_tensor("onesc", [128, 1], F32, kind="ExternalInput").ap()
    out = nc.dram_tensor("out", [KT, 128, TOK], F32, kind="ExternalOutput").ap()
    bins = [nc.dram_tensor(f"bin{g}", [128, 2], F32) for g in range(8)]
    bouts = [nc.dram_tensor(f"bout{g}", [256, 2], F32) for g in range(8)]

    B = Builder(nc)
    with ExitStack() as es:
        B.mksem(es, "din", "dgu", "ddn", "ts", "vs", "ss", "ost",
                "dcar", "ccs", "dgc")

        rg = [es.enter_context(nc.sbuf_tensor(f"rg{i}", [128, D], BF16)) for i in range(GU_RING)]
        s_xnown = [es.enter_context(nc.sbuf_tensor(f"xno{t}", [128, TOK], BF16)) for t in range(KT)]
        s_acc = [es.enter_context(nc.sbuf_tensor(f"acc{m}", [128, TOK], F32)) for m in range(MT)]
        s_id = es.enter_context(nc.sbuf_tensor("sid", [128, 128], F32))
        s_onesr = es.enter_context(nc.sbuf_tensor("sonesr", [1, 128], F32))
        s_onesc = es.enter_context(nc.sbuf_tensor("sonesc", [128, 1], F32))
        s_apar = es.enter_context(nc.sbuf_tensor("sapar", [128, 1], F32))
        s_cmask = es.enter_context(nc.sbuf_tensor("scmask", [128, 1], F32))
        s_spa = es.enter_context(nc.sbuf_tensor("sspa", [128, 1], F32))
        s_scol = es.enter_context(nc.sbuf_tensor("sscol", [128, 1], F32))
        s_kc = es.enter_context(nc.sbuf_tensor("skc", [1, 4], F32))
        psg = [es.enter_context(nc.psum_tensor(f"psg{i}", [128, 512], F32)) for i in range(4)]
        psd = [es.enter_context(nc.psum_tensor(f"psd{i}", [128, TOK], F32)) for i in range(2)]
        psg_free = [None] * 4
        psd_free = [None] * 2

        gu_consumed = [None] * N_GU
        dn_consumed = [None] * N_DN

        def gu_ready(j):
            return ("dgu", 16 * (j + 1) if j < 4 else 16 * (j + 3))

        gw_tok = ("dgu", 16 * 5)
        gv_tok = ("dgu", 16 * 6)

        def dn_ready(j):
            return ("ddn", 16 * (j + 1))

        def mm_job2(slotA, slotB, w_slab, kparts, rhsA_of_k, rhsB_of_k, extra_deps=()):
            for d in extra_deps:
                B.wait("tensor", d)
            tok = None
            for k in range(kparts):
                last = (k == kparts - 1)
                B.ins("tensor",
                      lambda e, s=slotA, w=w_slab, k=k, r=rhsA_of_k(k), la=last:
                      e.matmul(s, w[:, k * 128:(k + 1) * 128], r,
                               start=(k == 0), stop=la, skip_group_check=True))
                tok = B.ins("tensor",
                            lambda e, s=slotB, w=w_slab, k=k, r=rhsB_of_k(k), la=last:
                            e.matmul(s, w[:, k * 128:(k + 1) * 128], r,
                                     start=(k == 0), stop=la, skip_group_check=True),
                            "ts" if last else None, 1)
            return tok

        esW = ExitStack()
        rd = [esW.enter_context(nc.sbuf_tensor(f"rd{i}", [128, KI * 128], BF16)) for i in range(DN_RING)]
        s_gw = esW.enter_context(nc.sbuf_tensor("sgw", [128, 256], BF16))
        s_gv = esW.enter_context(nc.sbuf_tensor("sgv", [128, 16], BF16))
        s_sel = esW.enter_context(nc.sbuf_tensor("ssel", [16, D], BF16))
        s_comb = esW.enter_context(nc.sbuf_tensor("scomb", [16, TOK], BF16))
        s_sgb = esW.enter_context(nc.sbuf_tensor("ssgb", [128, TOK], BF16))
        s_combb = esW.enter_context(nc.sbuf_tensor("scombb", [128, TOK], BF16))

        esA = es.enter_context(ExitStack())
        s_x2 = [esA.enter_context(nc.sbuf_tensor(f"x2_{i}", [128, TOK], F32)) for i in range(2)]
        s_at = [esA.enter_context(nc.sbuf_tensor(f"at{i}", [128, 512], F32)) for i in range(2)]
        s_mean = esA.enter_context(nc.sbuf_tensor("smean", [1, TOK], F32))
        s_msq = esA.enter_context(nc.sbuf_tensor("smsq", [1, TOK], F32))
        s_var = esA.enter_context(nc.sbuf_tensor("svar", [1, TOK], F32))
        s_rstd = esA.enter_context(nc.sbuf_tensor("srstd", [1, TOK], F32))
        s_nmr = esA.enter_context(nc.sbuf_tensor("snmr", [1, TOK], F32))

        B.ins("vector", lambda e: e.memset(s_kc[0:1, 0:1], 1e-5), "vs", 1)
        B.ins("vector", lambda e: e.memset(s_kc[0:1, 1:2], 1.0 / D), "vs", 1)
        kc_tok = B.ins("vector", lambda e: e.memset(s_kc[0:1, 2:3], -0.5), "vs", 1)
        for (dst, src) in [(s_id, id128), (s_onesr, onesr), (s_onesc, onesc),
                           (s_apar, apar), (s_cmask, cmask)]:
            B.ins("sync", lambda e, d=dst, s=src: e.dma_start(out=d[:], in_=s), "din", 16)
        aux_tok = ("din", B.cnt["din"])

        xtok = [B.ins("sync",
                      lambda e, t=t, d=s_acc[t]: e.dma_start(out=d[:], in_=xin[t]),
                      "din", 16) for t in range(KT)]

        sq_tok = [None] * KT
        x2mm_tok = [None] * KT
        sumx_tok = [None] * KT
        B.wait("tensor", aux_tok)
        for t in range(KT):
            B.wait("scalar", xtok[t])
            if t >= 2:
                B.wait("scalar", x2mm_tok[t - 2])
            sq_tok[t] = B.ins("scalar",
                              lambda e, t=t, d=s_x2[t % 2]:
                              e.activation(out=d[:], in_=s_acc[t][:], func=AF.Square),
                              "ss", 1)
            B.wait("tensor", xtok[t])
            B.wait("tensor", sq_tok[t])
            for c in range(2):
                sumx_tok[t] = B.ins("tensor",
                                    lambda e, t=t, c=c:
                                    e.matmul(psd[0][0:1, c * 512:(c + 1) * 512], s_onesc[:],
                                             s_acc[t][:, c * 512:(c + 1) * 512],
                                             start=(t == 0), stop=(t == KT - 1)),
                                    "ts", 1)
                x2mm_tok[t] = B.ins("tensor",
                                    lambda e, t=t, c=c, d=s_x2[t % 2]:
                                    e.matmul(psd[1][0:1, c * 512:(c + 1) * 512], s_onesc[:],
                                             d[:, c * 512:(c + 1) * 512],
                                             start=(t == 0), stop=(t == KT - 1)),
                                    "ts", 1)
        sums_done = sumx_tok[KT - 1]
        sums_done2 = x2mm_tok[KT - 1]

        B.wait("scalar", sums_done)
        B.wait("scalar", sums_done2)
        B.wait("scalar", kc_tok)
        m_tok = B.ins("scalar", lambda e: e.activation(out=s_mean[:], in_=psd[0][0:1, :],
                                                       func=AF.Copy, scale=s_kc[0:1, 1:2]), "ss", 1)
        q_tok = B.ins("scalar", lambda e: e.activation(out=s_msq[:], in_=psd[1][0:1, :],
                                                       func=AF.Copy, scale=s_kc[0:1, 1:2]), "ss", 1)
        psd_free[0] = m_tok
        psd_free[1] = q_tok
        B.wait("vector", m_tok)
        B.wait("vector", q_tok)
        t1 = B.ins("vector", lambda e: e.tensor_mul(out=s_var[:], in0=s_mean[:],
                                                    in1=s_mean[:]), "vs", 1)
        B.wait("vector", t1)
        t2 = B.ins("vector", lambda e: e.tensor_sub(out=s_var[:], in0=s_msq[:],
                                                    in1=s_var[:]), "vs", 1)
        B.wait("scalar", t2)
        t3a = B.ins("scalar", lambda e: e.activation(out=s_rstd[:], in_=s_var[:],
                                                     func=AF.Ln, bias=s_kc[0:1, 0:1]), "ss", 1)
        B.wait("scalar", t3a)
        t3 = B.ins("scalar", lambda e: e.activation(out=s_rstd[:], in_=s_rstd[:],
                                                    func=AF.Exp, scale=s_kc[0:1, 2:3]), "ss", 1)
        B.wait("vector", t3)
        t5 = B.ins("vector", lambda e: e.scalar_tensor_tensor(
            out=s_nmr[:], in0=s_mean[:], scalar=-1.0, in1=s_rstd[:],
            op0=ALU.mult, op1=ALU.mult), "vs", 1)
        bc_tok = [None] * 4
        for c in range(2):
            B.wait("tensor", t3)
            bc_tok[c] = B.ins("tensor",
                              lambda e, c=c: e.matmul(psg[c][:, 0:512], s_onesr[:],
                                                      s_rstd[0:1, c * 512:(c + 1) * 512],
                                                      start=True, stop=True), "ts", 1)
        for c in range(2):
            B.wait("tensor", t5)
            bc_tok[2 + c] = B.ins("tensor",
                                  lambda e, c=c: e.matmul(psg[2 + c][:, 0:512], s_onesr[:],
                                                          s_nmr[0:1, c * 512:(c + 1) * 512],
                                                          start=True, stop=True), "ts", 1)
        xn_tok = [None] * KT
        for t in range(KT):
            B.wait("vector", xtok[t])
            for c in range(2):
                B.wait("vector", bc_tok[c])
                B.wait("vector", bc_tok[2 + c])
                u1 = B.ins("vector",
                           lambda e, t=t, c=c, tmp=s_at[t % 2]:
                           e.tensor_mul(out=tmp[:, 0:512],
                                        in0=s_acc[t][:, c * 512:(c + 1) * 512],
                                        in1=psg[c][:, 0:512]), "vs", 1)
                B.wait("vector", u1)
                u2 = B.ins("vector",
                           lambda e, t=t, c=c, tmp=s_at[t % 2]:
                           e.tensor_add(out=s_xnown[t][:, c * 512:(c + 1) * 512],
                                        in0=tmp[:, 0:512],
                                        in1=psg[2 + c][:, 0:512]), "vs", 1)
            xn_tok[t] = u2
        xn_all = xn_tok[KT - 1]
        for c in range(4):
            psg_free[c] = xn_all

        B.wait("scalar", aux_tok)
        p1 = B.ins("scalar", lambda e: e.activation(out=s_spa[:], in_=s_apar[:], func=AF.Abs), "ss", 1)
        B.wait("scalar", p1)
        p2 = B.ins("scalar", lambda e: e.activation(out=s_spa[:], in_=s_spa[:], func=AF.Exp,
                                                    scale=-1.0), "ss", 1)
        B.wait("scalar", p2)
        p3 = B.ins("scalar", lambda e: e.activation(out=s_spa[:], in_=s_spa[:], func=AF.Ln,
                                                    bias=1.0), "ss", 1)
        B.wait("scalar", p3)
        p4 = B.ins("scalar", lambda e: e.activation(out=s_scol[:], in_=s_apar[:], func=AF.Relu), "ss", 1)
        B.wait("scalar", p4)
        p5 = B.ins("scalar", lambda e: e.activation(out=s_scol[:], in_=s_scol[:], func=AF.Relu,
                                                    bias=s_spa[:, 0:1]), "ss", 1)
        B.wait("vector", p5)
        spa_tok = B.ins("vector", lambda e: e.tensor_scalar_mul(s_scol[:], s_scol[:], -8.0), "vs", 1)

        esA.close()

        esB = es.enter_context(ExitStack())
        s_hp = [esB.enter_context(nc.sbuf_tensor(f"hp{i}", [128, TOK], BF16)) for i in range(KI)]
        s_silu = [esB.enter_context(nc.sbuf_tensor(f"sil{i}", [128, 512], F32)) for i in range(2)]
        s_ut = [esB.enter_context(nc.sbuf_tensor(f"ut{i}", [128, 512], F32)) for i in range(2)]
        s_lg = esB.enter_context(nc.sbuf_tensor("slg", [16, TOK], F32))
        s_sg = esB.enter_context(nc.sbuf_tensor("ssg", [1, TOK], F32))
        s_ltc = esB.enter_context(nc.sbuf_tensor("sltc", [128, 128], F32))
        s_q = esB.enter_context(nc.sbuf_tensor("sq", [128, 128], F32))
        s_w = esB.enter_context(nc.sbuf_tensor("sw", [128, 128], F32))
        s_msk = esB.enter_context(nc.sbuf_tensor("smsk", [128, 128], F32))
        s_ctc = esB.enter_context(nc.sbuf_tensor("sctc", [128, 128], F32))
        s_nmax = esB.enter_context(nc.sbuf_tensor("snmax", [128, 8], F32))
        s_sum8 = esB.enter_context(nc.sbuf_tensor("ssum8", [128, 8], F32))
        s_rs8 = esB.enter_context(nc.sbuf_tensor("srs8", [128, 8], F32))
        s_m8 = esB.enter_context(nc.sbuf_tensor("sm8", [128, 8], F32))
        s_tr8 = esB.enter_context(nc.sbuf_tensor("str8", [128, 8], F32))

        sel_tok = B.ins("sync", lambda e: e.dma_start(out=s_sel[:], in_=seli), "din", 16)

        def xn_rhs(k, c):
            return s_xnown[k][:, c * 512:(c + 1) * 512]

        B.wait("tensor", gw_tok)
        if psd_free[0] is not None:
            B.wait("tensor", psd_free[0])
        lg_tok = None
        for c in range(2):
            for k in range(KT):
                B.wait("tensor", xn_tok[k])
                lg_tok = B.ins("tensor",
                               lambda e, c=c, k=k:
                               e.matmul(psd[0][0:16, c * 512:(c + 1) * 512],
                                        s_gw[:, k * 16:(k + 1) * 16], xn_rhs(k, c),
                                        start=(k == 0), stop=(k == KT - 1)),
                               "ts" if k == KT - 1 else None, 1)
        B.wait("scalar", lg_tok)
        lg_sb = B.ins("scalar", lambda e: e.copy(out=s_lg[:], in_=psd[0][0:16, :]), "ss", 1)
        psd_free[0] = lg_sb
        B.wait("tensor", gv_tok)
        if psd_free[1] is not None:
            B.wait("tensor", psd_free[1])
        sg_tok = None
        for c in range(2):
            for k in range(KT):
                sg_tok = B.ins("tensor",
                               lambda e, c=c, k=k:
                               e.matmul(psd[1][0:1, c * 512:(c + 1) * 512],
                                        s_gv[:, k:k + 1], xn_rhs(k, c),
                                        start=(k == 0), stop=(k == KT - 1)),
                               "ts" if k == KT - 1 else None, 1)
        B.wait("scalar", sg_tok)
        sg_row = B.ins("scalar", lambda e: e.activation(out=s_sg[:], in_=psd[1][0:1, :],
                                                        func=AF.Sigmoid), "ss", 1)
        psd_free[1] = sg_row
        sgb_tok = None
        for c in range(2):
            B.wait("tensor", sg_row)
            if psg_free[c] is not None:
                B.wait("tensor", psg_free[c])
            bt = B.ins("tensor", lambda e, c=c:
                       e.matmul(psg[c][:, 0:512], s_onesr[:], s_sg[0:1, c * 512:(c + 1) * 512],
                                start=True, stop=True), "ts", 1)
            B.wait("scalar", bt)
            sgb_tok = B.ins("scalar", lambda e, c=c:
                            e.copy(out=s_sgb[:, c * 512:(c + 1) * 512], in_=psg[c][:, 0:512]),
                            "ss", 1)
            psg_free[c] = sgb_tok

        ltc_tok = []
        for i in range(8):
            sl = i % 4
            B.wait("tensor", lg_sb)
            if psg_free[sl] is not None:
                B.wait("tensor", psg_free[sl])
            tt = B.ins("tensor",
                       lambda e, i=i, sl=sl:
                       e.transpose(psg[sl][:, 0:16], s_lg[0:16, i * 128:(i + 1) * 128],
                                   s_id[0:16, 0:16]), "ts", 1)
            B.wait("vector", tt)
            ct = B.ins("vector",
                       lambda e, i=i, sl=sl:
                       e.tensor_copy(out=s_ltc[:, i * 16:(i + 1) * 16], in_=psg[sl][:, 0:16]),
                       "vs", 1)
            psg_free[sl] = ct
            ltc_tok.append(ct)
        mtok = B.ins("vector", lambda e: e.memset(s_tr8[:], -2.0), "vs", 1)
        comb_tc_tok = []
        for i in range(8):
            isl = slice(i * 16, (i + 1) * 16)
            B.wait("vector", ltc_tok[i])
            r1 = B.ins("vector", lambda e, isl=isl, i=i:
                       e.reduce_max(out=s_nmax[:, i:i + 1], in_=s_ltc[:, isl], axis=AX.X,
                                    negate=True), "vs", 1)
            B.wait("scalar", r1)
            r2 = B.ins("scalar", lambda e, isl=isl, i=i:
                       e.activation(out=s_q[:, isl], in_=s_ltc[:, isl], func=AF.Exp,
                                    bias=s_nmax[:, i:i + 1]), "ss", 1)
            B.wait("vector", r2)
            r3 = B.ins("vector", lambda e, isl=isl:
                       e.max(out=s_m8[:], in_=s_q[:, isl]), "vs", 1)
            B.wait("vector", r3)
            B.wait("vector", mtok)
            r4 = B.ins("vector", lambda e: e.tensor_copy(out=s_tr8[:, 0:4], in_=s_m8[:, 0:4]),
                       "vs", 1)
            B.wait("vector", r4)
            r5 = B.ins("vector", lambda e, isl=isl:
                       e.match_replace(out=s_w[:, isl], in_to_replace=s_tr8[:],
                                       in_values=s_q[:, isl], imm_value=-1.0), "vs", 1)
            B.wait("vector", r5)
            r6 = B.ins("vector", lambda e, isl=isl:
                       e.tensor_tensor(out=s_msk[:, isl], in0=s_q[:, isl], in1=s_w[:, isl],
                                       op=ALU.not_equal), "vs", 1)
            B.wait("vector", r6)
            r7 = B.ins("vector", lambda e, isl=isl:
                       e.tensor_mul(out=s_ctc[:, isl], in0=s_q[:, isl], in1=s_msk[:, isl]),
                       "vs", 1)
            B.wait("vector", r7)
            r8 = B.ins("vector", lambda e, isl=isl, i=i:
                       e.reduce_sum(out=s_sum8[:, i:i + 1], in_=s_ctc[:, isl], axis=AX.X),
                       "vs", 1)
            B.wait("vector", r8)
            r9 = B.ins("vector", lambda e, i=i:
                       e.reciprocal(out=s_rs8[:, i:i + 1], in_=s_sum8[:, i:i + 1]), "vs", 1)
            B.wait("vector", r9)
            r10 = B.ins("vector", lambda e, isl=isl, i=i:
                        e.tensor_scalar_mul(s_ctc[:, isl], s_ctc[:, isl], s_rs8[:, i:i + 1]),
                        "vs", 1)
            comb_tc_tok.append(r10)
        if psd_free[0] is not None:
            B.wait("tensor", psd_free[0])
        tb_last = None
        for i in range(8):
            B.wait("tensor", comb_tc_tok[i])
            tb_last = B.ins("tensor",
                            lambda e, i=i:
                            e.transpose(psd[0][0:16, i * 128:(i + 1) * 128],
                                        s_ctc[:, i * 16:(i + 1) * 16], s_id[:]), "ts", 1)
        B.wait("scalar", tb_last)
        comb_tok = B.ins("scalar", lambda e: e.copy(out=s_comb[:], in_=psd[0][0:16, :]), "ss", 1)
        psd_free[0] = comb_tok

        acc_tok = [sums_done] * MT
        silu_cons = [None, None]
        ut_cons = [None, None]
        gslot_i = 0
        dslot_i = 0
        gu_j = 0
        dn_j = 0
        combb_tok = None

        for p in range(NPASS):
            is_exp = p >= 4
            eidx = p - 4
            if is_exp:
                for c in range(2):
                    sl = dslot_i % 2
                    dslot_i += 1
                    B.wait("tensor", comb_tok)
                    B.wait("tensor", sel_tok)
                    if psd_free[sl] is not None:
                        B.wait("tensor", psd_free[sl])
                    bt = B.ins("tensor",
                               lambda e, c=c, sl=sl, eidx=eidx:
                               e.matmul(psd[sl][:, 0:512],
                                        s_sel[:, eidx * 128:(eidx + 1) * 128],
                                        s_comb[0:16, c * 512:(c + 1) * 512],
                                        start=True, stop=True), "ts", 1)
                    B.wait("scalar", bt)
                    combb_tok = B.ins("scalar",
                                      lambda e, c=c, sl=sl:
                                      e.copy(out=s_combb[:, c * 512:(c + 1) * 512],
                                             in_=psd[sl][:, 0:512]), "ss", 1)
                    psd_free[sl] = combb_tok
            h_tok = [None] * KI
            for i in range(KI):
                g_slab_j = gu_j
                u_slab_j = gu_j + 1
                gu_j += 2
                slg0, slg1 = gslot_i % 4, (gslot_i + 1) % 4
                slu0, slu1 = (gslot_i + 2) % 4, (gslot_i + 3) % 4
                gslot_i += 4
                deps = [xn_all, gu_ready(g_slab_j)]
                for s_ in (slg0, slg1):
                    if psg_free[s_] is not None:
                        deps.append(psg_free[s_])
                gt = mm_job2(psg[slg0][:, 0:512], psg[slg1][:, 0:512],
                             rg[g_slab_j % GU_RING], KT,
                             lambda k: xn_rhs(k, 0), lambda k: xn_rhs(k, 1),
                             extra_deps=deps)
                gu_consumed[g_slab_j] = gt
                deps = [gu_ready(u_slab_j)]
                for s_ in (slu0, slu1):
                    if psg_free[s_] is not None:
                        deps.append(psg_free[s_])
                ut = mm_job2(psg[slu0][:, 0:512], psg[slu1][:, 0:512],
                             rg[u_slab_j % GU_RING], KT,
                             lambda k: xn_rhs(k, 0), lambda k: xn_rhs(k, 1),
                             extra_deps=deps)
                gu_consumed[u_slab_j] = ut
                for c in range(2):
                    slg_ = slg0 if c == 0 else slg1
                    slu_ = slu0 if c == 0 else slu1
                    B.wait("scalar", gt)
                    B.wait("scalar", silu_cons[c])
                    sv = B.ins("scalar",
                               lambda e, slg_=slg_, b=s_silu[c]:
                               e.activation(out=b[:], in_=psg[slg_][:, 0:512], func=AF.Silu),
                               "ss", 1)
                    psg_free[slg_] = sv
                    B.wait("vector", ut)
                    B.wait("vector", sv)
                    if is_exp:
                        B.wait("vector", ut_cons[c])
                        uv = B.ins("vector",
                                   lambda e, slu_=slu_, b=s_silu[c], o=s_ut[c]:
                                   e.scalar_tensor_tensor(out=o[:], in0=psg[slu_][:, 0:512],
                                                          scalar=0.0, in1=b[:],
                                                          op0=ALU.bypass, op1=ALU.mult),
                                   "vs", 1)
                        psg_free[slu_] = uv
                        silu_cons[c] = uv
                        B.wait("vector", uv)
                        B.wait("vector", combb_tok)
                        hv = B.ins("vector",
                                   lambda e, i=i, c=c, o=s_ut[c]:
                                   e.tensor_mul(out=s_hp[i][:, c * 512:(c + 1) * 512],
                                                in0=o[:], in1=s_combb[:, c * 512:(c + 1) * 512]),
                                   "vs", 1)
                        ut_cons[c] = hv
                    else:
                        hv = B.ins("vector",
                                   lambda e, i=i, c=c, slu_=slu_, b=s_silu[c]:
                                   e.scalar_tensor_tensor(
                                       out=s_hp[i][:, c * 512:(c + 1) * 512],
                                       in0=psg[slu_][:, 0:512], scalar=0.0, in1=b[:],
                                       op0=ALU.bypass, op1=ALU.mult), "vs", 1)
                        psg_free[slu_] = hv
                        silu_cons[c] = hv
                    h_tok[i] = hv
            for m in range(MT):
                d_slab_j = dn_j
                dn_j += 1
                sl = dslot_i % 2
                dslot_i += 1
                deps = [dn_ready(d_slab_j)] + h_tok
                if psd_free[sl] is not None:
                    deps.append(psd_free[sl])
                for d in deps:
                    B.wait("tensor", d)
                dtok = None
                for k in range(KI):
                    for c in range(2):
                        last = (k == KI - 1)
                        dtok = B.ins("tensor",
                                     lambda e, sl=sl, k=k, c=c, w=rd[d_slab_j % DN_RING], la=last:
                                     e.matmul(psd[sl][:, c * 512:(c + 1) * 512],
                                              w[:, k * 128:(k + 1) * 128],
                                              s_hp[k][:, c * 512:(c + 1) * 512],
                                              start=(k == 0), stop=la,
                                              skip_group_check=True),
                                     "ts" if (c == 1 and last) else None, 1)
                dn_consumed[d_slab_j] = dtok
                B.wait("vector", dtok)
                if acc_tok[m] is not None:
                    B.wait("vector", acc_tok[m])
                if p == 0:
                    ev = B.ins("vector",
                               lambda e, m=m, sl=sl: e.tensor_copy(out=s_acc[m][:],
                                                                   in_=psd[sl][:]), "vs", 1)
                else:
                    ev = B.ins("vector",
                               lambda e, m=m, sl=sl: e.tensor_add(out=s_acc[m][:],
                                                                  in0=psd[sl][:],
                                                                  in1=s_acc[m][:]), "vs", 1)
                acc_tok[m] = ev
                psd_free[sl] = ev
            if p == 3:
                for m in range(MT):
                    B.wait("vector", acc_tok[m])
                    B.wait("vector", sgb_tok)
                    acc_tok[m] = B.ins("vector",
                                       lambda e, m=m: e.tensor_mul(out=s_acc[m][:],
                                                                   in0=s_acc[m][:],
                                                                   in1=s_sgb[:]), "vs", 1)
        assert gu_j == NPASS * 22 and dn_j == N_DN
        ffn_ts_done = ("ts", B.cnt["ts"])

        silu_tok = [None] * MT
        for m in range(MT):
            B.wait("scalar", acc_tok[m])
            silu_tok[m] = B.ins("scalar",
                                lambda e, m=m: e.activation(out=s_acc[m][:], in_=s_acc[m][:],
                                                            func=AF.Silu), "ss", 1)

        for j in range(4):
            B.ins("gpsimd",
                  lambda e, j=j, sl=rg[j % GU_RING]: e.dma_start(out=sl[:], in_=gus[j]),
                  "dgu", 16)
        B.ins("gpsimd", lambda e: e.dma_start(out=s_gw[:], in_=gw), "dgu", 16)
        B.ins("gpsimd", lambda e: e.dma_start(out=s_gv[:], in_=gv), "dgu", 16)
        order = []
        for p in range(NPASS):
            for j in range(p * 22, (p + 1) * 22):
                if j >= 4:
                    order.append(("gu", j))
            for j in range(p * MT, (p + 1) * MT):
                order.append(("dn", j))
        for kind, j in order:
            if kind == "gu":
                B.wait("gpsimd", gu_consumed[j - GU_RING] if j >= GU_RING else None)
                B.ins("gpsimd",
                      lambda e, j=j, sl=rg[j % GU_RING]: e.dma_start(out=sl[:], in_=gus[j]),
                      "dgu", 16)
            else:
                B.wait("gpsimd", dn_consumed[j - DN_RING] if j >= DN_RING else None)
                B.ins("gpsimd",
                      lambda e, j=j, sl=rd[j % DN_RING]: e.dma_start(out=sl[:], in_=dns[j]),
                      "ddn", 16)
        for j in range(N_GU - 2 * KT, N_GU - 2 * KT + 4):
            B.wait("gpsimd", gu_consumed[j - GU_RING])
            B.ins("gpsimd",
                  lambda e, j=j, sl=rg[j % GU_RING]: e.dma_start(out=sl[:], in_=gus[j]),
                  "dgu", 16)
        esB.close()
        esW.close()

        esC = es.enter_context(ExitStack())
        s_gx = [esC.enter_context(nc.sbuf_tensor(f"sgx{i}", [128, TOK], F32)) for i in range(2)]
        s_ga = [esC.enter_context(nc.sbuf_tensor(f"sga{i}", [128, TOK], F32)) for i in range(2)]
        s_mt = esC.enter_context(nc.sbuf_tensor("smt", [128, TOK], F32))
        s_h = [esC.enter_context(nc.sbuf_tensor(f"sh{i}", [128, TOK], F32)) for i in range(4)]
        s_P = [esC.enter_context(nc.sbuf_tensor(f"sP{i}", [128, TOK], F32)) for i in range(4)]
        s_xo = [esC.enter_context(nc.sbuf_tensor(f"sxo{i}", [128, TOK], F32)) for i in range(4)]
        s_ob = [esC.enter_context(nc.sbuf_tensor(f"sob{i}", [128, TOK], F32)) for i in range(2)]
        s_car2 = esC.enter_context(nc.sbuf_tensor("scar2", [128, 2], F32))
        s_gcar = esC.enter_context(nc.sbuf_tensor("sgcar", [128, 16], F32))
        s_careff = esC.enter_context(nc.sbuf_tensor("scareff", [128, 16], F32))

        B.wait("sync", ffn_ts_done)
        B.wait("vector", ffn_ts_done)
        B.wait("scalar", ffn_ts_done)
        zero_ap = psd[0]
        B.wait("vector", psd_free[0])
        B.wait("vector", psd_free[1])
        zero_tok = B.ins("vector", lambda e: e.memset(zero_ap[:], 0.0), "vs", 1)

        WBASE = N_GU - 2 * KT
        sc_h = [None] * KT
        sc_P = [None] * KT
        b1_tok = [None] * KT
        xo_tok = [None] * KT
        o2_tok = [None] * KT
        ost_tok = [None] * KT
        cp_tok = [None] * KT
        car_dma = [None] * 8

        def emit_fixup(g):
            B.wait("vector", ("dgc", 16 * (g + 1)))
            ce = B.ins("vector",
                       lambda e, g=g: e.tensor_scalar_mul(s_careff[:, 2 * g:2 * g + 2],
                                                          s_gcar[:, 2 * g:2 * g + 2],
                                                          s_cmask[:, 0:1]), "vs", 1)
            for t in (2 * g, 2 * g + 1):
                B.wait("vector", ce)
                B.ins("vector",
                      lambda e, t=t: e.tensor_scalar_mul(s_P[t % 4][:], s_P[t % 4][:],
                                                         s_careff[:, t:t + 1]), "vs", 1)
                B.ins("vector",
                      lambda e, t=t: e.tensor_add(out=s_h[t % 4][:], in0=s_P[t % 4][:],
                                                  in1=s_h[t % 4][:]), "vs", 1)
                B.wait("vector", silu_tok[t])
                if t >= 2:
                    B.wait("vector", ost_tok[t - 2])
                B.ins("vector",
                      lambda e, t=t: e.tensor_mul(out=s_ob[t % 2][:], in0=s_h[t % 4][:],
                                                  in1=s_acc[t][:]), "vs", 1)
                B.wait("vector", xo_tok[t])
                o2_tok[t] = B.ins("vector",
                                  lambda e, t=t: e.tensor_add(out=s_ob[t % 2][:],
                                                              in0=s_ob[t % 2][:],
                                                              in1=s_xo[t % 4][:]), "vs", 1)
                B.wait("sync", o2_tok[t])
                ost_tok[t] = B.ins("sync",
                                   lambda e, t=t: e.dma_start(out=out[t], in_=s_ob[t % 2][:]),
                                   "ost", 16)

        for t in range(KT):
            g = t // 2
            jx, ja = WBASE + 2 * t, WBASE + 2 * t + 1
            deps = [gu_ready(jx), xn_all]
            for s_ in (0, 1):
                if psg_free[s_] is not None:
                    deps.append(psg_free[s_])
            gxj = mm_job2(psg[0][:, 0:512], psg[1][:, 0:512], rg[jx % GU_RING], KT,
                          lambda k: xn_rhs(k, 0), lambda k: xn_rhs(k, 1),
                          extra_deps=deps)
            gu_consumed[jx] = gxj
            deps = [gu_ready(ja)]
            for s_ in (2, 3):
                if psg_free[s_] is not None:
                    deps.append(psg_free[s_])
            gaj = mm_job2(psg[2][:, 0:512], psg[3][:, 0:512], rg[ja % GU_RING], KT,
                          lambda k: xn_rhs(k, 0), lambda k: xn_rhs(k, 1),
                          extra_deps=deps)
            gu_consumed[ja] = gaj

            if t >= 4:
                B.wait("sync", o2_tok[t - 4])
            xo_tok[t] = B.ins("sync",
                              lambda e, t=t: e.dma_start(out=s_xo[t % 4][:], in_=xin[t]),
                              "din", 16)

            for c in range(2):
                B.wait("scalar", gxj)
                if t >= 2:
                    B.wait("scalar", sc_h[t - 2])
                ev = B.ins("scalar",
                           lambda e, c=c, t=t:
                           e.activation(out=s_gx[t % 2][:, c * 512:(c + 1) * 512],
                                        in_=psg[c][:, 0:512], func=AF.Sigmoid), "ss", 1)
                psg_free[c] = ev
            for c in range(2):
                B.wait("scalar", gaj)
                if t >= 2:
                    B.wait("scalar", sc_P[t - 2])
                ev = B.ins("scalar",
                           lambda e, c=c, t=t:
                           e.activation(out=s_ga[t % 2][:, c * 512:(c + 1) * 512],
                                        in_=psg[2 + c][:, 0:512], func=AF.Sigmoid), "ss", 1)
                psg_free[2 + c] = ev
            B.wait("scalar", spa_tok)
            a_tok = B.ins("scalar",
                          lambda e, t=t: e.activation(out=s_ga[t % 2][:], in_=s_ga[t % 2][:],
                                                      func=AF.Exp, scale=s_scol[:, 0:1]), "ss", 1)
            if t >= 1:
                B.wait("scalar", b1_tok[t - 1])
            mq = B.ins("scalar",
                       lambda e, t=t: e.activation(out=s_mt[:], in_=s_ga[t % 2][:],
                                                   func=AF.Square), "ss", 1)
            B.wait("vector", mq)
            m2 = B.ins("vector",
                       lambda e: e.tensor_scalar(out=s_mt[:], in0=s_mt[:],
                                                 scalar1=-1.0, scalar2=1.0,
                                                 op0=ALU.mult, op1=ALU.add), "vs", 1)
            B.wait("vector", m2)
            m3 = B.ins("vector", lambda e: e.tensor_scalar_max(s_mt[:], s_mt[:], 1e-38), "vs", 1)
            B.wait("scalar", m3)
            m4 = B.ins("scalar",
                       lambda e: e.activation(out=s_mt[:], in_=s_mt[:], func=AF.Sqrt), "ss", 1)
            B.wait("vector", m4)
            b1 = B.ins("vector",
                       lambda e, t=t: e.tensor_mul(out=s_gx[t % 2][:], in0=s_mt[:],
                                                   in1=s_gx[t % 2][:]), "vs", 1)
            b1_tok[t] = b1
            B.wait("vector", b1)
            b2 = B.ins("vector",
                       lambda e, t=t: e.tensor_mul(out=s_gx[t % 2][:], in0=s_gx[t % 2][:],
                                                   in1=s_xnown[t][:]), "vs", 1)
            B.wait("vector", b2)
            B.wait("vector", a_tok)
            sc_h[t] = B.ins("vector",
                            lambda e, t=t: e.tensor_tensor_scan(out=s_h[t % 4][:],
                                                                data0=s_ga[t % 2][:],
                                                                data1=s_gx[t % 2][:],
                                                                initial=0.0, op0=ALU.mult,
                                                                op1=ALU.add), "vs", 1)
            B.wait("vector", zero_tok)
            sc_P[t] = B.ins("vector",
                            lambda e, t=t: e.tensor_tensor_scan(out=s_P[t % 4][:],
                                                                data0=s_ga[t % 2][:],
                                                                data1=zero_ap[:],
                                                                initial=1.0, op0=ALU.mult,
                                                                op1=ALU.add), "vs", 1)
            B.wait("vector", ("dcar", 16 * g))
            cp_tok[t] = B.ins("vector",
                              lambda e, t=t: e.tensor_copy(out=s_car2[:, t % 2:t % 2 + 1],
                                                           in_=s_h[t % 4][:, TOK - 1:TOK]),
                              "vs", 1)
            if t % 2 == 1:
                B.wait("sync", cp_tok[t - 1])
                B.wait("sync", cp_tok[t])
                car_dma[g] = B.ins("sync",
                                   lambda e, g=g: e.dma_start(out=bins[g].ap(),
                                                              in_=s_car2[:, 0:2]),
                                   "dcar", 16)
                for j in range(WBASE + 2 * (t + 1), min(WBASE + 2 * (t + 3), N_GU)):
                    B.wait("gpsimd", gu_consumed[j - GU_RING])
                    B.ins("gpsimd",
                          lambda e, j=j, sl=rg[j % GU_RING]: e.dma_start(out=sl[:], in_=gus[j]),
                          "dgu", 16)
                B.wait("gpsimd", ("dcar", 16 * (g + 1)))
                cct = B.ins("gpsimd",
                            lambda e, g=g: e.collective_compute(
                                "AllGather", ALU.bypass, replica_groups=RGROUPS,
                                ins=[bins[g].ap().opt()], outs=[bouts[g].ap().opt()]),
                            "ccs", 1)
                B.wait("gpsimd", cct)
                B.ins("gpsimd",
                      lambda e, g=g: e.dma_start(out=s_gcar[:, 2 * g:2 * g + 2],
                                                 in_=bouts[g].ap()[0:128, :]), "dgc", 16)
                if g >= 1:
                    emit_fixup(g - 1)
        emit_fixup(7)
        assert gu_j + 2 * KT == N_GU

        B.emit("sync", lambda e, v=B.cnt["ost"], s=B.sem["ost"]: e.wait_ge(s, v))

        block = es.enter_context(nc.Block())

        @block.sync
        def _(sync):
            for f in B.q["sync"]:
                f(sync)

        @block.gpsimd
        def _(g):
            for f in B.q["gpsimd"]:
                f(g)

        @block.tensor
        def _(te):
            for f in B.q["tensor"]:
                f(te)

        @block.vector
        def _(v):
            for f in B.q["vector"]:
                f(v)

        @block.scalar
        def _(s):
            for f in B.q["scalar"]:
                f(s)

    return nc


def _slab(W):
    Mo, Ki = W.shape
    A = W.reshape(Mo // 128, 128, Ki // 128, 128)
    return np.ascontiguousarray(A.transpose(0, 3, 2, 1)).reshape(Mo // 128, 128, Ki)


_BUILD_CACHE = {}


def _get_nc():
    if "nc" not in _BUILD_CACHE:
        _BUILD_CACHE["nc"] = build_nc()
    return _BUILD_CACHE["nc"]


def prep_in_maps(L0_attn_x, L0_attn_W_in, L0_attn_a_param, L0_ffn_gate_w,
                 L0_ffn_expert_gate_up, L0_ffn_expert_down, L0_ffn_shared_gate_up,
                 L0_ffn_shared_down, L0_ffn_shared_gate_vec):
    f32 = np.float32
    x = np.asarray(L0_attn_x, f32)
    W_in = np.asarray(L0_attn_W_in, f32)
    a_param = np.asarray(L0_attn_a_param, f32).reshape(128, 1)
    gate_w = np.asarray(L0_ffn_gate_w, f32)
    egu = np.asarray(L0_ffn_expert_gate_up, f32)
    edn = np.asarray(L0_ffn_expert_down, f32)
    sgu = np.asarray(L0_ffn_shared_gate_up, f32)
    sdn = np.asarray(L0_ffn_shared_down, f32)
    gvec = np.asarray(L0_ffn_shared_gate_vec, f32)

    sgu_slabs = _slab(sgu)
    gu_list = []
    for g in range(4):
        for i in range(KI):
            gu_list.append(sgu_slabs[11 * g + i])
            gu_list.append(sgu_slabs[44 + 11 * g + i])
    for e in range(E):
        es_ = _slab(egu[e])
        for i in range(KI):
            gu_list.append(es_[i])
            gu_list.append(es_[11 + i])
    win_slabs = _slab(W_in)
    for t in range(KT):
        gu_list.append(win_slabs[t])
        gu_list.append(win_slabs[KT + t])
    gus = np.ascontiguousarray(np.stack(gu_list))
    assert gus.shape == (N_GU, 128, D)

    sdn_slabs = _slab(sdn)
    dn_list = []
    for g in range(4):
        for m in range(MT):
            dn_list.append(sdn_slabs[m][:, 1408 * g:1408 * (g + 1)])
    for e in range(E):
        ed = _slab(edn[e])
        for m in range(MT):
            dn_list.append(ed[m])
    dns = np.ascontiguousarray(np.stack(dn_list))
    assert dns.shape == (N_DN, 128, KI * 128)

    gw_slab = np.ascontiguousarray(
        gate_w.reshape(16, 16, 128).transpose(2, 1, 0)).reshape(128, 256)
    gv_slab = np.ascontiguousarray(gvec.reshape(16, 128).T)
    sel = np.zeros((16, D), dtype=BF)
    for e in range(E):
        sel[e, e * 128:(e + 1) * 128] = 1.0
    id128 = np.eye(128, dtype=f32)
    onesr = np.ones((1, 128), f32)
    onesc = np.ones((128, 1), f32)

    in_maps = []
    for core in range(8):
        c, h = core >> 1, core & 1
        own = x[c, TOK * h:TOK * (h + 1), :]
        xt = np.ascontiguousarray(own.T).reshape(KT, 128, TOK)
        cm = np.full((128, 1), float(h), f32)
        in_maps.append(dict(xin=xt, gus=gus, dns=dns, gw=gw_slab, gv=gv_slab,
                            apar=a_param, cmask=cm, seli=sel, id128=id128,
                            onesr=onesr, onesc=onesc))
    return in_maps


def kernel(**inputs):
    in_maps = prep_in_maps(**inputs)
    nc = _get_nc()
    res = run_bass_kernel_spmd(nc, in_maps, core_ids=list(range(8)))
    x = np.asarray(inputs["L0_attn_x"], np.float32)
    outp = np.empty_like(x)
    for core in range(8):
        c, h = core >> 1, core & 1
        o = np.asarray(res.results[core]["out"])
        outp[c, TOK * h:TOK * (h + 1), :] = o.reshape(D, TOK).T
    return outp
